# version 02
# baseline (speedup 1.0000x reference)
import numpy as np
from contextlib import ExitStack

import concourse.bass as bass
import concourse.mybir as mybir
from concourse.bass_utils import run_bass_kernel_spmd
from concourse.bass_core_rust import IndirectOffsetOnAxis as _IOA

F32 = mybir.dt.float32
BF16 = mybir.dt.bfloat16
I32 = mybir.dt.int32
AF = mybir.ActivationFunctionType
ALU = mybir.AluOpType
AX = mybir.AxisListType

T = 2048
D = 1024
NH = 16
DK = 64
NE = 8
FF = 4096
NTI = 16

ENGINES = ("sync", "scalar", "vector", "tensor", "gpsimd")


class Buf:
    __slots__ = ("w", "r")
    def __init__(self):
        self.w = None
        self.r = []
    def rdeps(self):
        return [self.w] if self.w else []
    def wdeps(self):
        return ([self.w] if self.w else []) + self.r
    def wrote(self, tok):
        if tok is not None:
            self.w = tok
            self.r = []
    def read(self, tok):
        if tok is not None:
            self.r.append(tok)
            if len(self.r) > 16:
                best = {}
                for s, v in self.r:
                    if v > best.get(s, 0):
                        best[s] = v
                self.r = list(best.items())


class P:
    def __init__(self):
        self.q = {e: [] for e in ENGINES}
        self.cnt = {}
        self.lastw = {e: {} for e in ENGINES}
        self.sems = {}
        self.B = {}
        self.floor = []

    def buf(self, name):
        if name not in self.B:
            self.B[name] = Buf()
        return self.B[name]

    def fence(self):
        self.floor = [(s, v) for s, v in self.cnt.items()]
        self.B = {}

    def _wait_list(self, eng, deps):
        best = {}
        for d in deps:
            if d is None:
                continue
            s, v = d
            if v > best.get(s, 0):
                best[s] = v
        ws = []
        for s, v in best.items():
            if self.lastw[eng].get(s, 0) < v:
                ws.append((s, v))
                self.lastw[eng][s] = v
        return ws

    def raw_emit(self, eng, deps, f, sem, inc):
        ws = self._wait_list(eng, list(deps) + self.floor)
        tok = None
        if sem is not None:
            self.cnt[sem] = self.cnt.get(sem, 0) + inc
            tok = (sem, self.cnt[sem])
        sems = self.sems
        def g(e, ws=ws, f=f, tok=tok, inc=inc):
            for s, v in ws:
                e.wait_ge(sems[s], v)
            ins = f(e)
            if tok is not None:
                ins.then_inc(sems[tok[0]], inc)
        self.q[eng].append(g)
        return tok

    def op(self, eng, reads, writes, f, sem=None, inc=None, extra=()):
        deps = list(extra)
        for r in reads:
            deps += self.buf(r).rdeps()
        for w in writes:
            deps += self.buf(w).wdeps()
        if sem is None:
            sem = {"tensor": "pe", "vector": "dve", "scalar": "act",
                   "gpsimd": "pool", "sync": "ds"}[eng]
            inc = 16 if eng == "sync" else 1
        if inc is None:
            inc = 1
        tok = self.raw_emit(eng, deps, f, sem, inc)
        for w in writes:
            self.buf(w).wrote(tok)
        for r in reads:
            self.buf(r).read(tok)
        return tok

    def mm_group(self, reads, writes, fs, extra=()):
        deps = list(extra)
        for r in reads:
            deps += self.buf(r).rdeps()
        for w in writes:
            deps += self.buf(w).wdeps()
        tok = None
        for i, f in enumerate(fs):
            tok = self.raw_emit("tensor", deps if i == 0 else [], f,
                                "pe" if i == len(fs) - 1 else None, 1)
        for w in writes:
            self.buf(w).wrote(tok)
        for r in reads:
            self.buf(r).read(tok)
        return tok

    def barrier_wait(self, eng, deps):
        ws = self._wait_list(eng, deps)
        if not ws:
            return
        sems = self.sems
        def g(e, ws=ws):
            for s, v in ws:
                e.wait_ge(sems[s], v)
        self.q[eng].append(g)


def build_program(blocks=(0, 1, 2, 3)):
    nc = bass.Bass("TRN2", target_bir_lowering=False)

    X = nc.dram_tensor("x", [T, D], F32, kind="ExternalInput")
    wq = nc.dram_tensor("L0_attn_wq", [D, D], F32, kind="ExternalInput")
    bq = nc.dram_tensor("L0_attn_bq", [D], F32, kind="ExternalInput")
    wk = nc.dram_tensor("L0_attn_wk", [D, D], F32, kind="ExternalInput")
    bk = nc.dram_tensor("L0_attn_bk", [D], F32, kind="ExternalInput")
    wv = nc.dram_tensor("L0_attn_wv", [D, D], F32, kind="ExternalInput")
    bv = nc.dram_tensor("L0_attn_bv", [D], F32, kind="ExternalInput")
    wo = nc.dram_tensor("L0_attn_wo", [D, D], F32, kind="ExternalInput")
    bo = nc.dram_tensor("L0_attn_bo", [D], F32, kind="ExternalInput")
    cosT = nc.dram_tensor("L0_attn_cos", [3200, 64], F32, kind="ExternalInput")
    sinT = nc.dram_tensor("L0_attn_sin", [3200, 64], F32, kind="ExternalInput")
    gw0 = nc.dram_tensor("L0_ffn_gate_w", [NE, D], F32, kind="ExternalInput")
    gb0 = nc.dram_tensor("L0_ffn_gate_b", [NE], F32, kind="ExternalInput")
    w1 = nc.dram_tensor("L0_ffn_w1", [NE, FF, D], F32, kind="ExternalInput")
    b1 = nc.dram_tensor("L0_ffn_b1", [NE, FF], F32, kind="ExternalInput")
    w2 = nc.dram_tensor("L0_ffn_w2", [NE, D, FF], F32, kind="ExternalInput")
    b2 = nc.dram_tensor("L0_ffn_b2", [NE, D], F32, kind="ExternalInput")
    Wq1 = nc.dram_tensor("L1_attn_Wq", [D, D], F32, kind="ExternalInput")
    Wk1 = nc.dram_tensor("L1_attn_Wk", [D, D], F32, kind="ExternalInput")
    Wv1 = nc.dram_tensor("L1_attn_Wv", [D, D], F32, kind="ExternalInput")
    Wo1 = nc.dram_tensor("L1_attn_Wo", [D, D], F32, kind="ExternalInput")
    gW1 = nc.dram_tensor("L1_ffn_gate_W", [NE, D], F32, kind="ExternalInput")
    w1e = nc.dram_tensor("L1_ffn_w1_e", [NE, D, FF], F32, kind="ExternalInput")
    w3e = nc.dram_tensor("L1_ffn_w3_e", [NE, D, FF], F32, kind="ExternalInput")
    w2e = nc.dram_tensor("L1_ffn_w2_e", [NE, FF, D], F32, kind="ExternalInput")
    w1s = nc.dram_tensor("L1_ffn_w1_s", [NE, D, 512], F32, kind="ExternalInput")
    w3s = nc.dram_tensor("L1_ffn_w3_s", [NE, D, 512], F32, kind="ExternalInput")
    w2s = nc.dram_tensor("L1_ffn_w2_s", [NE, 512, D], F32, kind="ExternalInput")
    Y = nc.dram_tensor("y", [T, D], F32, kind="ExternalOutput")

    xres = [nc.dram_tensor(f"xres{i}", [T, D], F32) for i in range(2)]
    wq_b = nc.dram_tensor("wq_b", [D, D], BF16)
    wk_b = nc.dram_tensor("wk_b", [D, D], BF16)
    wv_b = nc.dram_tensor("wv_b", [D, D], BF16)
    wo_b = nc.dram_tensor("wo_b", [D, D], BF16)
    gw0_b = nc.dram_tensor("gw0_b", [NE, D], BF16)
    w1_b = nc.dram_tensor("w1_b", [NE, FF, D], BF16)
    w2_b = nc.dram_tensor("w2_b", [NE, D, FF], BF16)
    b1_b = nc.dram_tensor("b1_b", [NE, FF], BF16)
    Wq1_b = nc.dram_tensor("Wq1_b", [D, D], BF16)
    Wk1_b = nc.dram_tensor("Wk1_b", [D, D], BF16)
    Wv1_b = nc.dram_tensor("Wv1_b", [D, D], BF16)
    Wo1_b = nc.dram_tensor("Wo1_b", [D, D], BF16)
    gW1_b = nc.dram_tensor("gW1_b", [NE, D], BF16)
    w1e_b = nc.dram_tensor("w1e_b", [NE, D, FF], BF16)
    w3e_b = nc.dram_tensor("w3e_b", [NE, D, FF], BF16)
    w2e_b = nc.dram_tensor("w2e_b", [NE, FF, D], BF16)
    w1s_b = nc.dram_tensor("w1s_b", [NE, D, 512], BF16)
    w3s_b = nc.dram_tensor("w3s_b", [NE, D, 512], BF16)
    w2s_b = nc.dram_tensor("w2s_b", [NE, 512, D], BF16)
    ktm_d = nc.dram_tensor("ktm_d", [T, D], BF16)
    vtm_d = nc.dram_tensor("vtm_d", [T, D], BF16)
    idx_d = nc.dram_tensor("idx_d", [T, 1], I32)

    p = P()

    with ExitStack() as ctx:
        E = ctx.enter_context
        xn = [E(nc.sbuf_tensor(f"xn{i}", [128, T], BF16)) for i in range(8)]
        pool = [E(nc.sbuf_tensor(f"pl{i}", [128, T], BF16)) for i in range(20)]
        yac = [E(nc.sbuf_tensor(f"ya{i}", [128, D], F32)) for i in range(4)]
        scr = [E(nc.sbuf_tensor(f"sc{i}", [128, T], BF16)) for i in range(5)]
        wb = [E(nc.sbuf_tensor(f"wb{i}", [128, D], BF16)) for i in range(16)]
        xio = [E(nc.sbuf_tensor(f"xi{i}", [128, D], F32)) for i in range(3)]
        ident_b = E(nc.sbuf_tensor("idb", [128, 128], BF16))
        ones_f = E(nc.sbuf_tensor("onf", [1, 128], F32))
        ones_rb = E(nc.sbuf_tensor("onrb", [1, 128], BF16))
        ones_cb = E(nc.sbuf_tensor("oncb", [128, 1], BF16))
        rows = E(nc.sbuf_tensor("rows", [1, 1160], F32))
        rows_b = E(nc.sbuf_tensor("rowsb", [1, 3200], BF16))
        b1c = E(nc.sbuf_tensor("b1c", [128, 256], BF16))
        gwt = E(nc.sbuf_tensor("gwt", [128, 64], BF16))
        gbb = E(nc.sbuf_tensor("gbb", [128, 8], F32))
        tri8 = E(nc.sbuf_tensor("tri8", [128, 64], F32))
        logit = [E(nc.sbuf_tensor(f"lg{i}", [128, 8], F32)) for i in range(16)]
        msk = [E(nc.sbuf_tensor(f"mk{i}", [128, 8], F32)) for i in range(16)]
        col = [E(nc.sbuf_tensor(f"cl{i}", [128, 1], F32)) for i in range(8)]
        stat = [E(nc.sbuf_tensor(f"st{i}", [128, 16], F32)) for i in range(8)]
        colb = [E(nc.sbuf_tensor(f"cb{i}", [128, 1], BF16)) for i in range(2)]
        tri128 = E(nc.sbuf_tensor("tri128", [128, 128], F32))
        cmps = E(nc.sbuf_tensor("cmps", [128, 128], F32))
        imps = E(nc.sbuf_tensor("imps", [128, 128], BF16))
        cmp8 = cmps[:, 0:64]
        cmp8b = cmps[:, 64:128]
        coli = [E(nc.sbuf_tensor(f"ci{i}", [128, 1], I32)) for i in range(2)]

        ps_big = E(nc.psum_tensor("psB", [128, 2048], F32))
        ps_a = E(nc.psum_tensor("psa", [128, 512], F32))
        ps_b = E(nc.psum_tensor("psb", [128, 512], F32))
        ps_t0 = E(nc.psum_tensor("pst0", [128, 1024], BF16))
        ps_t1 = E(nc.psum_tensor("pst1", [128, 1024], BF16))
        PSV = {"ps_a": ps_a, "ps_b": ps_b}
        def psBv(i):
            return ps_big[:, 512 * i:512 * (i + 1)]

        cast_tok = {}
        def cast2(dst, src, name):
            cast_tok[name] = p.op("gpsimd", [], [], lambda g, d=dst, s=src:
                                  g.dma_start(d[:, :], s[:, :]),
                                  sem="dgc", inc=16)
        def cast3(dst, src, name):
            tks = []
            for e in range(src.shape[0]):
                tks.append(p.op("gpsimd", [], [],
                                lambda g, d=dst, s=src, e=e:
                                g.dma_start(d[e, :, :], s[e, :, :]),
                                sem="dgc", inc=16))
            cast_tok[name] = tks[-1]
        cast2(wq_b, wq, "wq"); cast2(wk_b, wk, "wk")
        cast2(wv_b, wv, "wv"); cast2(wo_b, wo, "wo")
        cast2(gw0_b, gw0, "gw0"); cast2(b1_b, b1, "b1")
        if 1 in blocks:
            cast3(w1_b, w1, "w1"); cast3(w2_b, w2, "w2")
        if 2 in blocks:
            cast2(Wq1_b, Wq1, "Wq1"); cast2(Wk1_b, Wk1, "Wk1")
            cast2(Wv1_b, Wv1, "Wv1"); cast2(Wo1_b, Wo1, "Wo1")
        if 3 in blocks:
            cast2(gW1_b, gW1, "gW1")
            cast3(w1e_b, w1e, "w1e"); cast3(w3e_b, w3e, "w3e")
            cast3(w2e_b, w2e, "w2e")
            cast3(w1s_b, w1s, "w1s"); cast3(w3s_b, w3s, "w3s")
            cast3(w2s_b, w2s, "w2s")

        p.op("gpsimd", [], ["yac0"], lambda g: g.iota(
            yac[0][:, 0:128], [[1, 128]], channel_multiplier=0,
            allow_small_or_imprecise_dtypes=True))
        p.op("gpsimd", [], ["yac0"], lambda g: g.iota(
            yac[0][:, 128:256], [[0, 128]], channel_multiplier=1,
            allow_small_or_imprecise_dtypes=True))
        p.op("vector", ["yac0"], ["ident_b"], lambda v: v.tensor_tensor(
            ident_b[:, :], yac[0][:, 0:128], yac[0][:, 128:256], ALU.is_equal))
        p.op("gpsimd", [], ["ones_f"], lambda g: g.memset(ones_f[:, :], 1.0))
        p.op("gpsimd", [], ["ones_rb"], lambda g: g.memset(ones_rb[:, :], 1.0))
        p.op("gpsimd", [], ["ones_cb"], lambda g: g.memset(ones_cb[:, :], 1.0))
        p.op("gpsimd", [], ["yac0"], lambda g: g.iota(
            yac[0][:, 256:320], [[0, 8], [1, 8]], channel_multiplier=0,
            allow_small_or_imprecise_dtypes=True))
        p.op("gpsimd", [], ["yac0"], lambda g: g.iota(
            yac[0][:, 320:384], [[1, 8], [0, 8]], channel_multiplier=0,
            allow_small_or_imprecise_dtypes=True))
        p.op("vector", ["yac0"], ["tri8"], lambda v: v.tensor_tensor(
            tri8[:, :], yac[0][:, 256:320], yac[0][:, 320:384], ALU.is_lt))
        p.op("vector", ["yac0"], ["tri128"], lambda v: v.tensor_tensor(
            tri128[:, :], yac[0][:, 0:128], yac[0][:, 128:256], ALU.is_lt))

        ROFF = {"slot": 0, "gb0": 1024, "cos": 1032, "sin": 1096}
        RB = {"c2": 0, "s2": 1024, "b2": 2048}
        def ldrow(dst_off, src_ap, n):
            return p.op("sync", [], ["rows"], lambda s, o=dst_off, a=src_ap,
                        n=n: s.dma_start(rows[0:1, o:o+n], a), sem="dsr",
                        inc=16)
        ldrow(ROFF["gb0"], gb0[:].unsqueeze(0), 8)
        ldrow(ROFF["cos"], cosT[T:T+1, :], 64)
        ldrow(ROFF["sin"], sinT[T:T+1, :], 64)

        def bcast(row_off, n, out_ap, out_name, scale=None, src_name="rows"):
            toks = []
            row_t = rows if src_name == "rows" else rows_b
            one_t = ones_f if src_name == "rows" else ones_rb
            one_n = "ones_f" if src_name == "rows" else "ones_rb"
            for o in range(0, n, 512):
                w = min(512, n - o)
                p.mm_group([src_name, one_n], ["ps_b"], [
                    lambda t, o=o, w=w: t.matmul(
                        ps_b[:, 0:w], one_t[0:1, :],
                        row_t[0:1, row_off+o:row_off+o+w],
                        start=True, stop=True)])
                if scale is None:
                    tke = p.op("vector", ["ps_b"], [out_name],
                               lambda v, o=o, w=w: v.tensor_copy(
                                   out_ap[:, o:o+w], ps_b[:, 0:w]))
                else:
                    tke = p.op("vector", ["ps_b"], [out_name],
                               lambda v, o=o, w=w: v.tensor_scalar_mul(
                                   out_ap[:, o:o+w], ps_b[:, 0:w], scale))
                toks.append(tke)
            return toks[-1]

        def emit_ln(src, blk, src_tok=None):
            extra0 = [src_tok] if src_tok else []
            for i in range(NTI):
                j = i % 3
                sj = i % 2
                if sj == 0:
                    cc = [col[k][:, 0:1] for k in range(7)]
                    ck = [f"col{k}" for k in range(7)]
                else:
                    cc = [stat[k][:, 15:16] for k in range(7)]
                    ck = [f"lno{k}" for k in range(7)]
                sqs = scr[2][:, 1024:2048] if sj == 0 else scr[3][:, 1024:2048]
                sqn = "scr2" if sj == 0 else "scr3"
                p.op("sync", [], [f"xio{j}"],
                     lambda s, i=i, j=j: s.dma_start(
                         xio[j][:, :], src[128*i:128*(i+1), :]),
                     sem=f"dsx{blk}", inc=16, extra=extra0)
                p.op("vector", [f"xio{j}"], [ck[0]],
                     lambda v, j=j, cc=cc: v.reduce_sum(cc[0], xio[j][:, :],
                                                        AX.X))
                p.op("vector", [ck[0]], [ck[1]],
                     lambda v, cc=cc: v.tensor_scalar_mul(cc[1], cc[0],
                                                          1.0 / D))
                p.op("scalar", [f"xio{j}"], [sqn, ck[2]],
                     lambda a, j=j, cc=cc, sqs=sqs: a.activation(
                         sqs, xio[j][:, :], AF.Square,
                         accum_out=cc[2]))
                p.op("vector", [ck[1]], [ck[3]],
                     lambda v, cc=cc: v.tensor_tensor(cc[3], cc[1],
                                                      cc[1], ALU.mult))
                p.op("vector", [ck[2], ck[3]], [ck[4]],
                     lambda v, cc=cc: v.scalar_tensor_tensor(
                         cc[4], cc[2], 1.0 / D, cc[3],
                         ALU.mult, ALU.subtract))
                p.op("vector", [], [ck[4]],
                     lambda v, cc=cc: v.tensor_single_scalar(
                         cc[4], cc[4], 1e-5, ALU.add))
                p.op("scalar", [ck[4]], [ck[5]],
                     lambda a, cc=cc: a.activation(cc[5], cc[4],
                                                   AF.Sqrt))
                p.op("vector", [ck[5]], [ck[6]],
                     lambda v, cc=cc: v.reciprocal(cc[6], cc[5]))
                p.op("vector", [ck[1], ck[6], f"xio{j}"], [f"scr{sj}"],
                     lambda v, j=j, sj=sj, cc=cc: v.tensor_scalar(
                         scr[sj][:, 0:1024], xio[j][:, :],
                         cc[1], cc[6],
                         ALU.subtract, ALU.mult))
                pst = ps_t0 if sj == 0 else ps_t1
                pstn = f"ps_t{sj}"
                fs = []
                for c in range(8):
                    fs.append(lambda t, c=c, sj=sj, pst=pst: t.transpose(
                        pst[:, 128*c:128*(c+1)],
                        scr[sj][:, 128*c:128*(c+1)], ident_b[:, :]))
                p.mm_group([f"scr{sj}", "ident_b"], [pstn], fs)
                for c in range(8):
                    p.op("vector", [pstn], [f"xn{c}"],
                         lambda v, c=c, i=i, pst=pst: v.tensor_copy(
                             xn[c][:, 128*i:128*(i+1)],
                             pst[:, 128*c:128*(c+1)]))

        def xn_names():
            return [f"xn{c}" for c in range(8)]

        def residual_store(src, dst, blk, tc, acc_name, acc_ap):
            j = tc % 2
            p.op("sync", [], [f"xio{j}"], lambda s, tc=tc, j=j:
                 s.dma_start(xio[j][:, :], src[128*tc:128*(tc+1), :]),
                 sem=f"dsx2{blk}", inc=16)
            p.op("vector", [f"xio{j}", acc_name], ["xio2"],
                 lambda v, j=j: v.tensor_tensor(
                     xio[2][:, :], xio[j][:, :], acc_ap, ALU.add))
            return p.op("sync", ["xio2"], [f"xout{blk}"],
                        lambda s, tc=tc: s.dma_start(
                            dst[128*tc:128*(tc+1), :], xio[2][:, :]),
                        sem=f"dso{blk}", inc=16)

        def emit_l0_attn(src, dst, blk, src_tok=None):
            ldrow(ROFF["slot"], bq[:].unsqueeze(0), 1024)
            emit_ln(src, blk, src_tok)
            for c in range(8):
                p.op("sync", [], [f"wb{c}"], lambda s, c=c: s.dma_start(
                    wb[c][:, :], wq_b[:, 128*c:128*(c+1)], transpose=True),
                    sem=f"dsw{blk}", inc=16, extra=[cast_tok["wq"]])
            for c in range(8):
                p.op("sync", [], [f"wb{8+c}"], lambda s, c=c: s.dma_start(
                    wb[8+c][:, :], wk_b[:, 128*c:128*(c+1)], transpose=True),
                    sem=f"dsw{blk}", inc=16, extra=[cast_tok["wk"]])
            for c in range(8):
                p.op("sync", [], [f"pl{16+c//2}"], lambda s, c=c: s.dma_start(
                    pool[16+c//2][:, 1024*(c % 2):1024*(c % 2)+1024],
                    wv_b[:, 128*c:128*(c+1)], transpose=True),
                    sem=f"dsw{blk}", inc=16, extra=[cast_tok["wv"]])

            bcast(ROFF["slot"], 1024, scr[3][:, 0:1024], "scr3", scale=0.125)
            ldrow(ROFF["slot"], bk[:].unsqueeze(0), 1024)
            bcast(ROFF["slot"], 1024, scr[3][:, 1024:2048], "scr3")
            ldrow(ROFF["slot"], bv[:].unsqueeze(0), 1024)
            bcast(ROFF["slot"], 1024, scr[1][:, 0:1024], "scr1")
            p.op("gpsimd", [], ["rows_b"], lambda g: g.dma_start(
                rows_b[0:1, RB["b2"]:RB["b2"]+1024], bo[:].unsqueeze(0)),
                sem="dgc", inc=16)
            for r in range(16):
                p.op("vector", ["rows"], ["rows_b"],
                     lambda v, r=r: v.tensor_copy(
                         rows_b[0:1, RB["c2"]+64*r:RB["c2"]+64*(r+1)],
                         rows[0:1, ROFF["cos"]:ROFF["cos"]+64]))
            for r in range(16):
                p.op("vector", ["rows"], ["rows_b"],
                     lambda v, r=r: v.tensor_copy(
                         rows_b[0:1, RB["s2"]+64*r:RB["s2"]+64*(r+1)],
                         rows[0:1, ROFF["sin"]:ROFF["sin"]+64]))
            p.op("vector", [], ["rows_b"], lambda v: v.tensor_scalar_mul(
                rows_b[0:1, RB["s2"]:RB["s2"]+1024].rearrange(
                    "o (n two) -> o n two", two=2)[:, :, 0:1],
                rows_b[0:1, RB["s2"]:RB["s2"]+1024].rearrange(
                    "o (n two) -> o n two", two=2)[:, :, 0:1], -1.0))
            bcast(RB["c2"], 1024, scr[4][:, 0:1024], "scr4",
                  src_name="rows_b")
            bcast(RB["s2"], 1024, scr[4][:, 1024:2048], "scr4",
                  src_name="rows_b")

            qt = [(pool[i // 2], (i % 2) * 1024, f"pl{i//2}")
                  for i in range(16)]
            kt = [(pool[8 + i // 2], (i % 2) * 1024, f"pl{8+i//2}")
                  for i in range(16)]
            vt = [(wb[i], 0, f"wb{i}") for i in range(16)]

            def project(dst_tiles, w_names, w_tiles, kind):
                for tc in range(NTI):
                    for half in range(2):
                        psn = "ps_a" if half == 0 else "ps_b"
                        psv = PSV[psn]
                        fs = []
                        for ic in range(8):
                            wt_, woff = w_tiles[ic]
                            fs.append(lambda t, ic=ic, tc=tc, half=half,
                                      wt_=wt_, woff=woff, psv=psv: t.matmul(
                                          psv[:, :],
                                          xn[ic][:, 128*tc:128*(tc+1)],
                                          wt_[:, woff+512*half:
                                              woff+512*(half+1)],
                                          start=(ic == 0), stop=(ic == 7)))
                        p.mm_group(xn_names() + w_names, [psn], fs)
                        dt_, doff, dname = dst_tiles[tc]
                        if kind == "q":
                            p.op("vector", [psn, "scr3"], [dname],
                                 lambda v, dt_=dt_, doff=doff, half=half,
                                 psv=psv: v.scalar_tensor_tensor(
                                     dt_[:, doff+512*half:doff+512*(half+1)],
                                     psv[:, :], 0.125,
                                     scr[3][:, 512*half:512*(half+1)],
                                     ALU.mult, ALU.add))
                        elif kind == "k":
                            p.op("vector", [psn, "scr3"], [dname],
                                 lambda v, dt_=dt_, doff=doff, half=half,
                                 psv=psv: v.tensor_tensor(
                                     dt_[:, doff+512*half:doff+512*(half+1)],
                                     psv[:, :],
                                     scr[3][:, 1024+512*half:
                                            1024+512*(half+1)],
                                     ALU.add))
                        else:
                            p.op("vector", [psn, "scr1"], [dname],
                                 lambda v, dt_=dt_, doff=doff, half=half,
                                 psv=psv: v.tensor_tensor(
                                     dt_[:, doff+512*half:doff+512*(half+1)],
                                     psv[:, :],
                                     scr[1][:, 512*half:512*(half+1)],
                                     ALU.add))

            project(qt, [f"wb{c}" for c in range(8)],
                    [(wb[c], 0) for c in range(8)], "q")
            project(kt, [f"wb{8+c}" for c in range(8)],
                    [(wb[8+c], 0) for c in range(8)], "k")
            project(vt, [f"pl{16+c}" for c in range(4)],
                    [(pool[16+c//2], (c % 2)*1024) for c in range(8)], "v")

            def rope(tiles):
                for i in range(NTI):
                    t_, off, nm = tiles[i]
                    p.op("vector", [nm, "scr4"], ["scr1"],
                         lambda v, t_=t_, off=off: v.tensor_copy(
                             scr[1][:, 1024:2048].rearrange(
                                 "p (n two) -> p n two", two=2)[:, :, 0:1],
                             t_[:, off:off+1024].rearrange(
                                 "p (n two) -> p n two", two=2)[:, :, 1:2]))
                    p.op("vector", [nm], ["scr1"],
                         lambda v, t_=t_, off=off: v.tensor_copy(
                             scr[1][:, 1024:2048].rearrange(
                                 "p (n two) -> p n two", two=2)[:, :, 1:2],
                             t_[:, off:off+1024].rearrange(
                                 "p (n two) -> p n two", two=2)[:, :, 0:1]))
                    p.op("vector", [nm, "scr4"], ["scr1"],
                         lambda v, t_=t_, off=off: v.tensor_tensor(
                             scr[1][:, 0:1024], t_[:, off:off+1024],
                             scr[4][:, 0:1024], ALU.mult))
                    p.op("vector", ["scr4"], ["scr1"],
                         lambda v: v.tensor_tensor(
                             scr[1][:, 1024:2048], scr[1][:, 1024:2048],
                             scr[4][:, 1024:2048], ALU.mult))
                    p.op("vector", ["scr1"], [nm],
                         lambda v, t_=t_, off=off: v.tensor_tensor(
                             t_[:, off:off+1024], scr[1][:, 0:1024],
                             scr[1][:, 1024:2048], ALU.add))
            rope(qt)
            rope(kt)

            for c in range(8):
                p.op("sync", [f"pl{16+c//2}"], [f"pl{16+c//2}"],
                     lambda s, c=c: s.dma_start(
                         pool[16+c//2][:, 1024*(c % 2):1024*(c % 2)+1024],
                         wo_b[:, 128*c:128*(c+1)], transpose=True),
                     sem=f"dsw{blk}", inc=16, extra=[cast_tok["wo"]])

            for h in range(NH):
                qh_t, qh_off, qh_n = qt[h]
                kh_t, kh_off, kh_n = kt[h]
                vh_n = f"wb{h}"
                for g2 in range(2):
                    pst = ps_t0 if g2 == 0 else ps_t1
                    pstn = f"ps_t{g2}"
                    fs = []
                    for jj in range(8):
                        jy = 8 * g2 + jj
                        fs.append(lambda t, jj=jj, jy=jy, qh_t=qh_t,
                                  qh_off=qh_off, pst=pst: t.transpose(
                                      pst[0:64, 128*jj:128*(jj+1)],
                                      qh_t[:, qh_off+64*jy:qh_off+64*(jy+1)],
                                      ident_b[:, :]))
                    p.mm_group([qh_n, "ident_b"], [pstn], fs)
                    p.op("vector", [pstn], ["scr0"],
                         lambda v, g2=g2, pst=pst: v.tensor_copy(
                             scr[0][0:64, 1024*g2:1024*(g2+1)],
                             pst[0:64, 0:1024]))
                for g2 in range(2):
                    pst = ps_t0 if g2 == 0 else ps_t1
                    pstn = f"ps_t{g2}"
                    fs = []
                    for jj in range(8):
                        jy = 8 * g2 + jj
                        fs.append(lambda t, jj=jj, jy=jy, kh_t=kh_t,
                                  kh_off=kh_off, pst=pst: t.transpose(
                                      pst[0:64, 128*jj:128*(jj+1)],
                                      kh_t[:, kh_off+64*jy:kh_off+64*(jy+1)],
                                      ident_b[:, :]))
                    p.mm_group([kh_n, "ident_b"], [pstn], fs)
                    p.op("vector", [pstn], ["scr2"],
                         lambda v, g2=g2, pst=pst: v.tensor_copy(
                             scr[2][0:64, 1024*g2:1024*(g2+1)],
                             pst[0:64, 0:1024]))

                for jq in range(NH):
                    pb = scr[1] if jq % 2 == 0 else scr[4]
                    pbn = "scr1" if jq % 2 == 0 else "scr4"
                    fs = []
                    for jk in range(NH):
                        fs.append(lambda t, jk=jk, jq=jq: t.matmul(
                            ps_big[:, 128*jk:128*(jk+1)],
                            scr[0][0:64, 128*jq:128*(jq+1)],
                            scr[2][0:64, 128*jk:128*(jk+1)],
                            start=True, stop=True))
                    p.mm_group(["scr0", "scr2"], ["ps_big"], fs)
                    p.op("vector", ["ps_big"], ["col0"],
                         lambda v: v.reduce_max(col[0][:, :], ps_big[:, :],
                                                AX.X, negate=True))
                    p.op("scalar", ["ps_big", "col0"], [pbn, "col1"],
                         lambda a, pb=pb: a.activation(
                             pb[:, :], ps_big[:, :], AF.Exp,
                             bias=col[0][:, 0:1], accum_out=col[1][:, :]))
                    p.op("vector", ["col1"], ["col2"],
                         lambda v: v.reciprocal(col[2][:, :], col[1][:, :]))
                    p.op("vector", ["col2"], [pbn],
                         lambda v, pb=pb: v.tensor_scalar_mul(
                             pb[:, :], pb[:, :], col[2][:, 0:1]))
                    for gq in range(2):
                        pst = ps_t0 if gq == 0 else ps_t1
                        pstn = f"ps_t{gq}"
                        fs = []
                        for cc in range(8):
                            c = gq * 8 + cc
                            fs.append(lambda t, c=c, cc=cc, pst=pst, pb=pb:
                                      t.transpose(
                                          pst[:, 128*cc:128*(cc+1)],
                                          pb[:, 128*c:128*(c+1)],
                                          ident_b[:, :]))
                        p.mm_group([pbn, "ident_b"], [pstn], fs)
                        p.op("vector", [pstn], [pbn],
                             lambda v, gq=gq, pst=pst, pb=pb: v.tensor_copy(
                                 pb[:, 1024*gq:1024*(gq+1)],
                                 pst[:, 0:1024]))
                    pon = "ps_a" if jq % 2 == 0 else "ps_b"
                    pov = PSV[pon]
                    fs = []
                    for jk in range(NH):
                        fs.append(lambda t, jk=jk, pov=pov, h=h, pb=pb:
                                  t.matmul(
                                      pov[0:64, 0:128],
                                      wb[h][:, 64*jk:64*(jk+1)],
                                      pb[:, 128*jk:128*(jk+1)],
                                      start=(jk == 0), stop=(jk == 15)))
                    p.mm_group([pbn, vh_n], [pon], fs)
                    p.op("vector", [pon], ["scr3"],
                         lambda v, jq=jq, pov=pov: v.tensor_copy(
                             scr[3][0:64, 1024*(jq//8)+128*(jq % 8):
                                    1024*(jq//8)+128*(jq % 8)+128],
                             pov[0:64, 0:128]))
                for halfj in range(2):
                    dst_ap = xn[h // 2][64*(h % 2):64*(h % 2)+64, :]\
                        .rearrange("p (tq j) -> p tq j", j=16)[
                            :, :, 8*halfj:8*halfj+8]
                    src_ap = scr[3][0:64, 1024*halfj:1024*(halfj+1)]\
                        .rearrange("p (j q) -> p q j", j=8)
                    p.op("vector", ["scr3"], [f"xn{h//2}"],
                         lambda v, dst_ap=dst_ap, src_ap=src_ap:
                         v.tensor_copy(dst_ap, src_ap))

            st = None
            for tc in range(NTI):
                j = tc % 2
                p.op("sync", [], [f"xio{j}"], lambda s, tc=tc, j=j:
                     s.dma_start(xio[j][:, :], src[128*tc:128*(tc+1), :]),
                     sem=f"dsx2{blk}", inc=16)
                for half in range(2):
                    psn = "ps_a" if half == 0 else "ps_b"
                    psv = PSV[psn]
                    fs = []
                    for ic in range(8):
                        fs.append(lambda t, ic=ic, tc=tc, half=half, psv=psv:
                                  t.matmul(
                                      psv[:, :],
                                      xn[ic][:, 128*tc:128*(tc+1)],
                                      pool[16+ic//2][:, 1024*(ic % 2) +
                                                     512*half:
                                                     1024*(ic % 2) +
                                                     512*(half+1)],
                                      start=(ic == 0), stop=False))
                    fs.append(lambda t, half=half, psv=psv: t.matmul(
                        psv[:, :], ones_rb[0:1, :],
                        rows_b[0:1, RB["b2"]+512*half:RB["b2"]+512*(half+1)],
                        start=False, stop=True))
                    p.mm_group(xn_names() + [f"pl{16+c}" for c in range(4)] +
                               ["ones_rb", "rows_b"], [psn], fs)
                    p.op("vector", [psn, f"xio{j}"], ["xio2"],
                         lambda v, half=half, psv=psv, j=j: v.tensor_tensor(
                             xio[2][:, 512*half:512*(half+1)], psv[:, :],
                             xio[j][:, 512*half:512*(half+1)], ALU.add))
                st = p.op("sync", ["xio2"], [f"xout{blk}"],
                          lambda s, tc=tc: s.dma_start(
                              dst[128*tc:128*(tc+1), :], xio[2][:, :]),
                          sem=f"dso{blk}", inc=16)
            return st

        def emit_l0_ffn(src, dst, blk, src_tok=None):
            emit_ln(src, blk, src_tok)
            p.op("sync", [], ["b1c"], lambda s: s.dma_start(
                b1c[:, :], b1_b[:, :].rearrange("e (j p) -> (e j) p", p=128),
                transpose=True), sem="dsr", inc=16,
                extra=[cast_tok["b1"]])
            p.op("sync", [], ["scr3"], lambda s: s.dma_start(
                scr[3][0:8, 0:1024], gw0_b[:, :]), sem=f"dsw{blk}", inc=16,
                extra=[cast_tok["gw0"]])
            fs = []
            for c in range(8):
                fs.append(lambda t, c=c: t.transpose(
                    ps_t0[:, 8*c:8*(c+1)], scr[3][0:8, 128*c:128*(c+1)],
                    ident_b[0:8, 0:8]))
            p.mm_group(["scr3", "ident_b"], ["ps_t0"], fs)
            p.op("vector", ["ps_t0"], ["gwt"], lambda v: v.tensor_copy(
                gwt[:, :], ps_t0[:, 0:64]))
            bcast(ROFF["gb0"], 8, gbb[:, :], "gbb")
            for tc in range(NTI):
                fs = []
                for ic in range(8):
                    fs.append(lambda t, ic=ic, tc=tc: t.matmul(
                        ps_a[:, 0:8], xn[ic][:, 128*tc:128*(tc+1)],
                        gwt[:, 8*ic:8*(ic+1)],
                        start=(ic == 0), stop=(ic == 7)))
                p.mm_group(xn_names() + ["gwt"], ["ps_a"], fs)
                p.op("vector", ["ps_a", "gbb"], [f"lg{tc}"],
                     lambda v, tc=tc: v.tensor_tensor(
                         logit[tc][:, :], ps_a[:, 0:8], gbb[:, :], ALU.add))
                for e in range(8):
                    p.op("vector", [f"lg{tc}"], ["cmp8"],
                         lambda v, tc=tc, e=e: v.tensor_scalar(
                             cmp8[:, 8*e:8*(e+1)], logit[tc][:, :],
                             logit[tc][:, e:e+1], None, ALU.is_gt))
                    p.op("vector", [f"lg{tc}"], ["cmp8b"],
                         lambda v, tc=tc, e=e: v.tensor_scalar(
                             cmp8b[:, 8*e:8*(e+1)], logit[tc][:, :],
                             logit[tc][:, e:e+1], None, ALU.is_equal))
                p.op("vector", ["tri8"], ["cmp8b"],
                     lambda v: v.tensor_tensor(cmp8b, cmp8b,
                                               tri8[:, :], ALU.mult))
                p.op("vector", ["cmp8b"], ["cmp8"],
                     lambda v: v.tensor_tensor(cmp8, cmp8,
                                               cmp8b, ALU.add))
                p.op("vector", ["cmp8"], [f"mk{tc}"],
                     lambda v, tc=tc: v.reduce_sum(
                         msk[tc][:, :],
                         cmp8.rearrange("p (e j) -> p e j", j=8),
                         AX.X))
                p.op("vector", [], [f"mk{tc}"],
                     lambda v, tc=tc: v.tensor_single_scalar(
                         msk[tc][:, :], msk[tc][:, :], 0.5, ALU.is_lt))

            h_tiles = [(pool[fc // 2], (fc % 2) * 1024, f"pl{fc//2}")
                       for fc in range(8)]
            w1set = [[(wb[i], 0, f"wb{i}") for i in range(8)],
                     [(pool[4 + i], 0, f"pl{4+i}") for i in range(8)]]
            w2set = [[(wb[8 + i], 0, f"wb{8+i}") for i in range(8)],
                     [(pool[12 + i], 0, f"pl{12+i}") for i in range(8)]]
            st = None
            nfcb = 0
            for ts in range(4):
                for e in range(NE):
                    p.op("gpsimd", [], ["rows_b"],
                         lambda g, e=e: g.dma_start(
                             rows_b[0:1, RB["b2"]:RB["b2"]+1024],
                             b2[e:e+1, :]), sem="dgc", inc=16)
                    for fcb in range(4):
                        f0 = fcb * 1024
                        par = nfcb % 2
                        nfcb += 1
                        w1t = w1set[par]
                        w2t = w2set[par]
                        for ic in range(8):
                            wt_, woff, wnm = w1t[ic]
                            p.op("sync", [], [wnm],
                                 lambda s, wt_=wt_, woff=woff, e=e, f0=f0,
                                 ic=ic: s.dma_start(
                                     wt_[:, woff:woff+1024],
                                     w1_b[e, f0:f0+1024, 128*ic:128*(ic+1)],
                                     transpose=True),
                                 sem=f"dsw{blk}", inc=16,
                                 extra=[cast_tok["w1"]])
                        for fc in range(8):
                            wt_, woff, wnm = w2t[fc]
                            p.op("sync", [], [wnm],
                                 lambda s, wt_=wt_, woff=woff, fc=fc, e=e,
                                 f0=f0: s.dma_start(
                                     wt_[:, woff:woff+1024],
                                     w2_b[e, :, f0+128*fc:f0+128*(fc+1)],
                                     transpose=True),
                                 sem=f"dsw{blk}", inc=16,
                                 extra=[cast_tok["w2"]])
                        for fc in range(8):
                            ht_, hoff, hnm = h_tiles[fc]
                            psn = "ps_a" if fc % 2 == 0 else "ps_b"
                            psv = PSV[psn]
                            fs = []
                            for ic in range(8):
                                wt_, woff, wnm = w1t[ic]
                                fs.append(lambda t, ic=ic, fc=fc, wt_=wt_,
                                          woff=woff, ts=ts, psv=psv:
                                          t.matmul(
                                              psv[:, :],
                                              wt_[:, woff+128*fc:
                                                  woff+128*(fc+1)],
                                              xn[ic][:, 512*ts:512*(ts+1)],
                                              start=(ic == 0),
                                              stop=(ic == 7)))
                            p.mm_group(xn_names() +
                                       [w1t[i][2] for i in range(8)],
                                       [psn], fs)
                            jcol = 32 * e + 8 * fcb + fc
                            p.op("scalar", [psn, "b1c"], [hnm],
                                 lambda a, ht_=ht_, hoff=hoff,
                                 psv=psv, jcol=jcol: a.activation(
                                     ht_[:, hoff:hoff+512],
                                     psv[:, :], AF.Relu,
                                     bias=b1c[:, jcol:jcol+1]))
                        for tc2 in range(4):
                            tcg = ts * 4 + tc2
                            for dh in range(2):
                                pv = 2 * (tc2 % 2) + dh
                                psn = f"psB{pv}"
                                fs = []
                                for fc in range(8):
                                    ht_, hoff, hnm = h_tiles[fc]
                                    wt_, woff, wnm = w2t[fc]
                                    last = (fc == 7 and fcb != 3)
                                    fs.append(lambda t, fc=fc, tc2=tc2,
                                              dh=dh, pv=pv, ht_=ht_,
                                              hoff=hoff, last=last,
                                              wt_=wt_, woff=woff:
                                              t.matmul(
                                                  psBv(pv),
                                                  ht_[:, hoff+128*tc2:
                                                      hoff+128*(tc2+1)],
                                                  wt_[:, woff+512*dh:
                                                      woff+512*(dh+1)],
                                                  start=(fc == 0),
                                                  stop=last))
                                if fcb == 3:
                                    fs.append(lambda t, dh=dh, pv=pv:
                                              t.matmul(
                                                  psBv(pv),
                                                  ones_rb[0:1, :],
                                                  rows_b[0:1,
                                                         RB["b2"]+512*dh:
                                                         RB["b2"] +
                                                         512*(dh+1)],
                                                  start=False, stop=True))
                                rd = [h_tiles[fc][2] for fc in range(8)] + \
                                    [w2t[i][2] for i in range(8)] + \
                                    (["ones_rb", "rows_b"] if fcb == 3
                                     else [])
                                p.mm_group(rd, [psn], fs)
                                first = (e == 0 and fcb == 0)
                                if first:
                                    p.op("vector", [psn, f"mk{tcg}"],
                                         [f"yac{tc2}"],
                                         lambda v, tc2=tc2, dh=dh, pv=pv,
                                         tcg=tcg, e=e: v.tensor_scalar(
                                             yac[tc2][:, 512*dh:512*(dh+1)],
                                             psBv(pv),
                                             msk[tcg][:, e:e+1], None,
                                             ALU.mult))
                                else:
                                    p.op("vector", [psn, f"mk{tcg}",
                                                    f"yac{tc2}"],
                                         [f"yac{tc2}"],
                                         lambda v, tc2=tc2, dh=dh, pv=pv,
                                         tcg=tcg, e=e:
                                         v.scalar_tensor_tensor(
                                             yac[tc2][:, 512*dh:512*(dh+1)],
                                             psBv(pv), msk[tcg][:, e:e+1],
                                             yac[tc2][:, 512*dh:512*(dh+1)],
                                             ALU.mult, ALU.add))
                for tc2 in range(4):
                    tcg = ts * 4 + tc2
                    st = residual_store(src, dst, blk, tcg, f"yac{tc2}",
                                        yac[tc2][:, :])
            return st

        def emit_l1_attn(src, dst, blk, src_tok=None):
            emit_ln(src, blk, src_tok)
            for c in range(8):
                p.op("sync", [], [f"wb{c}"], lambda s, c=c: s.dma_start(
                    wb[c][:, :], Wq1_b[128*c:128*(c+1), :]),
                    sem=f"dsw{blk}", inc=16, extra=[cast_tok["Wq1"]])
            for c in range(8):
                p.op("sync", [], [f"wb{8+c}"], lambda s, c=c: s.dma_start(
                    wb[8+c][:, :], Wk1_b[128*c:128*(c+1), :]),
                    sem=f"dsw{blk}", inc=16, extra=[cast_tok["Wk1"]])

            def fm_proj(dst_tiles, wtiles, wnames, scale=None):
                for oc in range(8):
                    for nh in range(4):
                        psn = "ps_a" if nh % 2 == 0 else "ps_b"
                        psv = PSV[psn]
                        fs = []
                        for ic in range(8):
                            fs.append(lambda t, ic=ic, oc=oc, nh=nh, psv=psv,
                                      wtiles=wtiles: t.matmul(
                                          psv[:, :],
                                          wtiles[ic][:, 128*oc:128*(oc+1)],
                                          xn[ic][:, 512*nh:512*(nh+1)],
                                          start=(ic == 0), stop=(ic == 7)))
                        p.mm_group(xn_names() + wnames, [psn], fs)
                        dt_, dnm = dst_tiles[oc]
                        if scale is None:
                            p.op("vector", [psn], [dnm],
                                 lambda v, dt_=dt_, nh=nh, psv=psv:
                                 v.tensor_copy(
                                     dt_[:, 512*nh:512*(nh+1)], psv[:, :]))
                        else:
                            p.op("vector", [psn], [dnm],
                                 lambda v, dt_=dt_, nh=nh, psv=psv:
                                 v.tensor_scalar_mul(
                                     dt_[:, 512*nh:512*(nh+1)], psv[:, :],
                                     scale))

            def tm_proj_to_dram(wtiles, wnames, dram):
                for tc in range(NTI):
                    for half in range(2):
                        psn = "ps_a" if half == 0 else "ps_b"
                        psv = PSV[psn]
                        fs = []
                        for ic in range(8):
                            fs.append(lambda t, ic=ic, tc=tc, half=half,
                                      psv=psv, wtiles=wtiles: t.matmul(
                                          psv[:, :],
                                          xn[ic][:, 128*tc:128*(tc+1)],
                                          wtiles[ic][:, 512*half:
                                                      512*(half+1)],
                                          start=(ic == 0), stop=(ic == 7)))
                        p.mm_group(xn_names() + wnames, [psn], fs)
                        p.op("vector", [psn], ["scr4"],
                             lambda v, half=half, psv=psv, tc=tc:
                             v.tensor_copy(
                                 scr[4][:, 1024*(tc % 2)+512*half:
                                        1024*(tc % 2)+512*(half+1)],
                                 psv[:, :]))
                    p.op("sync", ["scr4"], [f"tmout{blk}"],
                         lambda s, tc=tc, dram=dram: s.dma_start(
                             dram[128*tc:128*(tc+1), :],
                             scr[4][:, 1024*(tc % 2):1024*(tc % 2)+1024]),
                         sem=f"dst{blk}", inc=16)

            qfm = [(pool[i], f"pl{i}") for i in range(8)]
            kfm = [(pool[8+i], f"pl{8+i}") for i in range(8)]
            vfm = [(pool[16+i], f"pl{16+i}") for i in range(4)]
            wqn = [f"wb{c}" for c in range(8)]
            wkn = [f"wb{8+c}" for c in range(8)]
            fm_proj(qfm, [wb[c] for c in range(8)], wqn, scale=0.125)
            fm_proj(kfm, [wb[8+c] for c in range(8)], wkn)
            tm_proj_to_dram([wb[8+c] for c in range(8)], wkn, ktm_d)
            for c in range(8):
                p.op("sync", [], [f"wb{c}"], lambda s, c=c: s.dma_start(
                    wb[c][:, :], Wv1_b[128*c:128*(c+1), :]),
                    sem=f"dsw{blk}", inc=16, extra=[cast_tok["Wv1"]])
            tm_proj_to_dram([wb[c] for c in range(8)], wqn, vtm_d)

            def vfm_stage(stage):
                for oci in range(4):
                    oc = 4 * stage + oci
                    for nh in range(4):
                        psn = "ps_a" if nh % 2 == 0 else "ps_b"
                        psv = PSV[psn]
                        fs = []
                        for ic in range(8):
                            fs.append(lambda t, ic=ic, oc=oc, nh=nh, psv=psv:
                                      t.matmul(
                                          psv[:, :],
                                          wb[ic][:, 128*oc:128*(oc+1)],
                                          xn[ic][:, 512*nh:512*(nh+1)],
                                          start=(ic == 0), stop=(ic == 7)))
                        p.mm_group(xn_names() + wqn, [psn], fs)
                        p.op("vector", [psn], [f"pl{16+oci}"],
                             lambda v, oci=oci, nh=nh, psv=psv:
                             v.tensor_copy(
                                 pool[16+oci][:, 512*nh:512*(nh+1)],
                                 psv[:, :]))
            vfm_stage(0)

            p.op("gpsimd", [], ["stat5"], lambda g: g.memset(
                stat[5][:, :], 0.0))
            p.op("gpsimd", [], ["stat6"], lambda g: g.memset(
                stat[6][:, :], 0.0))
            n_infl = 0
            for h in range(NH):
                h2 = 64 * (h % 2)
                qf_t, qf_n = qfm[h // 2]
                kf_t, kf_n = kfm[h // 2]
                for tq in range(NTI):
                    pb = scr[4] if tq % 2 == 0 else scr[2]
                    pbn = "scr4" if tq % 2 == 0 else "scr2"
                    fs = []
                    for sl in range(4):
                        fs.append(lambda t, sl=sl, tq=tq, h2=h2, qf_t=qf_t,
                                  kf_t=kf_t: t.matmul(
                                      ps_big[:, 512*sl:512*(sl+1)],
                                      qf_t[h2:h2+64, 128*tq:128*(tq+1)],
                                      kf_t[h2:h2+64, 512*sl:512*(sl+1)],
                                      start=True, stop=True))
                    p.mm_group([qf_n, kf_n], ["ps_big"], fs)
                    p.op("vector", ["ps_big"], ["col0"],
                         lambda v: v.reduce_max(col[0][:, :], ps_big[:, :],
                                                AX.X, negate=True))
                    p.op("scalar", ["ps_big", "col0"], [pbn, "col1"],
                         lambda a, pb=pb: a.activation(
                             pb[:, :], ps_big[:, :], AF.Exp,
                             bias=col[0][:, 0:1], accum_out=col[1][:, :]))
                    p.op("vector", ["ps_big", pbn, "col0"], ["scr1",
                                                             "stat1"],
                         lambda v, tq=tq, pb=pb: v.scalar_tensor_tensor(
                             scr[1][:, 0:2048],
                             ps_big[:, :], col[0][:, 0:1], pb[:, :],
                             ALU.add, ALU.mult,
                             accum_out=stat[1][:, tq:tq+1]))
                    p.op("vector", ["col1"], [f"stat0"],
                         lambda v, tq=tq: v.tensor_copy(
                             stat[0][:, tq:tq+1], col[1][:, :]))
                    p.op("vector", ["col1"], ["col2"],
                         lambda v: v.reciprocal(col[2][:, :], col[1][:, :]))
                    p.op("vector", ["col2"], ["colb0"],
                         lambda v: v.tensor_copy(colb[0][:, :],
                                                 col[2][:, :]))
                    fs = []
                    for kc in range(NTI):
                        st_fl = (n_infl == 0 and kc == 0)
                        sp_fl = (n_infl == NH * NTI - 1 and kc == NTI - 1)
                        fs.append(lambda t, kc=kc, st_fl=st_fl, sp_fl=sp_fl,
                                  pb=pb: t.matmul(
                                      ps_a[:, kc:kc+1],
                                      pb[:, 128*kc:128*(kc+1)],
                                      colb[0][:, 0:1],
                                      start=st_fl, stop=sp_fl))
                    n_infl += 1
                    p.mm_group([pbn, "colb0"], ["ps_a"], fs)
                p.op("scalar", ["stat0"], ["stat2"],
                     lambda a: a.activation(stat[2][:, :], stat[0][:, :],
                                            AF.Ln))
                p.op("vector", ["stat0"], ["stat3"],
                     lambda v: v.reciprocal(stat[3][:, :], stat[0][:, :]))
                p.op("vector", ["stat1", "stat3"], ["stat4"],
                     lambda v: v.tensor_tensor(stat[4][:, :], stat[1][:, :],
                                               stat[3][:, :], ALU.mult))
                p.op("vector", ["stat2", "stat4"], ["stat4"],
                     lambda v: v.tensor_tensor(stat[4][:, :], stat[2][:, :],
                                               stat[4][:, :],
                                               ALU.subtract))
                p.op("vector", ["stat4"], ["stat5"],
                     lambda v: v.tensor_tensor(stat[5][:, :], stat[5][:, :],
                                               stat[4][:, :], ALU.add))
                p.op("vector", ["stat3"], ["stat6"],
                     lambda v: v.tensor_tensor(stat[6][:, :], stat[6][:, :],
                                               stat[3][:, :], ALU.add))
            p.op("vector", ["ps_a"], ["stat7"],
                 lambda v: v.tensor_copy(stat[7][:, :], ps_a[:, 0:16]))
            p.op("vector", ["stat5", "stat6"], ["stat0"],
                 lambda v: v.scalar_tensor_tensor(
                     stat[0][:, :], stat[5][:, :], -0.4 / 16.0,
                     stat[6][:, :], ALU.mult, ALU.bypass))
            p.op("vector", ["stat6", "stat0"], ["stat0"],
                 lambda v: v.scalar_tensor_tensor(
                     stat[0][:, :], stat[6][:, :], 0.3 / 16.0,
                     stat[0][:, :], ALU.mult, ALU.add))
            p.op("vector", ["stat7", "stat0"], ["stat0"],
                 lambda v: v.scalar_tensor_tensor(
                     stat[0][:, :], stat[7][:, :], 0.3 / 16.0,
                     stat[0][:, :], ALU.mult, ALU.add))
            p.fence()
            p.op("vector", ["stat0"], ["imps_c"],
                 lambda v: v.tensor_copy(imps[:, 0:16], stat[0][:, :]))
            p.op("vector", ["imps_c"], ["stat0"],
                 lambda v: v.tensor_copy(stat[0][:, :], imps[:, 0:16]))
            p.mm_group(["imps_c", "ident_b"], ["ps_t0"], [
                lambda t: t.transpose(ps_t0[0:16, 0:128], imps[:, 0:16],
                                      ident_b[:, :])])
            p.op("vector", ["ps_t0"], ["imps"],
                 lambda v: v.tensor_copy(
                     imps[0:16, 0:128], ps_t0[0:16, 0:128]))
            for b_ in range(NTI):
                p.op("gpsimd", ["imps"], ["rows_b"],
                     lambda g, b_=b_: g.dma_start(
                         rows_b[0:1, 2048:2176],
                         imps[b_:b_+1, 0:128]), sem="dgi", inc=16)
                p.mm_group(["rows_b", "ones_rb"], ["ps_b"], [
                    lambda t: t.matmul(ps_b[:, 0:128], ones_rb[0:1, :],
                                       rows_b[0:1, 2048:2176],
                                       start=True, stop=True)])
                yb = yac[b_ // 8]
                p.op("vector", ["ps_b"], [f"ybc{b_//8}"],
                     lambda v, b_=b_, yb=yb: v.tensor_copy(
                         yb[:, 128*(b_ % 8):128*(b_ % 8)+128], ps_b[:, 0:128]))
            for ta in range(NTI):
                p.op("gpsimd", [], ["col3"], lambda g: g.memset(
                    col[3][:, :], 0.0))
                for tb in range(NTI):
                    yb = yac[tb // 8]
                    bap = yb[:, 128*(tb % 8):128*(tb % 8)+128]
                    p.op("vector", [f"ybc{tb//8}", "stat0"],
                         ["cmps", "col4"],
                         lambda v, bap=bap, ta=ta: v.tensor_scalar(
                             cmps[:, :], bap, stat[0][:, ta:ta+1], 0.0,
                             ALU.is_gt, ALU.add,
                             accum_out=col[4][:, :]))
                    p.op("vector", ["col4", "col3"], ["col3"],
                         lambda v: v.tensor_tensor(col[3][:, :],
                                                   col[3][:, :],
                                                   col[4][:, :], ALU.add))
                    if tb < ta:
                        p.op("vector", [f"ybc{tb//8}", "stat0"],
                             ["cmps", "col4"],
                             lambda v, bap=bap, ta=ta: v.tensor_scalar(
                                 cmps[:, :], bap, stat[0][:, ta:ta+1], 0.0,
                                 ALU.is_equal, ALU.add,
                                 accum_out=col[4][:, :]))
                        p.op("vector", ["col4", "col3"], ["col3"],
                             lambda v: v.tensor_tensor(col[3][:, :],
                                                       col[3][:, :],
                                                       col[4][:, :],
                                                       ALU.add))
                    elif tb == ta:
                        p.op("vector", [f"ybc{tb//8}", "stat0"], ["cmps"],
                             lambda v, bap=bap, ta=ta: v.tensor_scalar(
                                 cmps[:, :], bap, stat[0][:, ta:ta+1],
                                 None, ALU.is_equal))
                        p.op("vector", ["cmps", "tri128"], ["cmps"],
                             lambda v: v.tensor_tensor(cmps[:, :],
                                                       cmps[:, :],
                                                       tri128[:, :],
                                                       ALU.mult))
                        p.op("vector", ["cmps"], ["col4"],
                             lambda v: v.reduce_sum(col[4][:, :],
                                                    cmps[:, :], AX.X))
                        p.op("vector", ["col4", "col3"], ["col3"],
                             lambda v: v.tensor_tensor(col[3][:, :],
                                                       col[3][:, :],
                                                       col[4][:, :],
                                                       ALU.add))
                p.op("vector", ["col3"], ["coli0"],
                     lambda v: v.tensor_copy(coli[0][:, :], col[3][:, :]))
                p.op("gpsimd", [], ["coli1"],
                     lambda g, ta=ta: g.iota(
                         coli[1][:, :], [[0, 1]], base=128*ta,
                         channel_multiplier=1))
                p.op("gpsimd", ["coli0", "coli1"], [f"idxd{blk}"],
                     lambda g: g.indirect_dma_start(
                         idx_d[:, :], _IOA(coli[0][:, 0:1], 0),
                         coli[1][:, 0:1], None), sem="dgi", inc=16)
            p.fence()
            p.op("gpsimd", [], ["coli0"], lambda g: g.dma_start(
                coli[0][:, :], idx_d[0:128, :]), sem="dgi", inc=16)
            p.op("gpsimd", [], ["coli1"], lambda g: g.dma_start(
                coli[1][0:76, :], idx_d[128:204, :]), sem="dgi", inc=16)
            p.op("gpsimd", ["coli0"], ["scr3"],
                 lambda g: g.indirect_dma_start(
                     scr[3][:, 0:1024], None,
                     ktm_d[0:128, :], _IOA(coli[0][:, 0:1], 0)),
                 sem="dgi", inc=16)
            p.op("gpsimd", ["coli1"], ["scr3"],
                 lambda g: g.indirect_dma_start(
                     scr[3][0:76, 1024:2048], None,
                     ktm_d[0:76, :], _IOA(coli[1][0:76, 0:1], 0)),
                 sem="dgi", inc=16)
            p.op("gpsimd", ["coli0"], ["scr1"],
                 lambda g: g.indirect_dma_start(
                     scr[1][:, 0:1024], None,
                     vtm_d[0:128, :], _IOA(coli[0][:, 0:1], 0)),
                 sem="dgi", inc=16)
            p.op("gpsimd", ["coli1"], ["scr1"],
                 lambda g: g.indirect_dma_start(
                     scr[1][0:76, 1024:2048], None,
                     vtm_d[0:76, :], _IOA(coli[1][0:76, 0:1], 0)),
                 sem="dgi", inc=16)
            for c in range(8):
                p.op("sync", [], [f"wb{8+c}"], lambda s, c=c: s.dma_start(
                    wb[8+c][:, :], Wo1_b[128*c:128*(c+1), :]),
                    sem=f"dsw{blk}", inc=16, extra=[cast_tok["Wo1"]])
            p.fence()

            def compress(src_t, src_n, h, dst_ap, dst_n):
                h2 = 64 * (h % 2)
                p.op("vector", [src_n], ["scr2"],
                     lambda v: v.tensor_copy(scr[2][0:64, 0:2048],
                                             src_t[h2:h2+64, :]))
                sl_ap = scr[2][0:64, :]
                p.op("scalar", ["scr2"], ["scr4"],
                     lambda a: a.activation(scr[4][0:64, 0:2048],
                                            sl_ap, AF.Square))
                for sl in range(4):
                    p.mm_group(["scr4", "ones_cb"], ["ps_b"], [
                        lambda t, sl=sl: t.matmul(
                            ps_b[0:1, 0:512], ones_cb[0:64, 0:1],
                            scr[4][0:64, 512*sl:512*(sl+1)],
                            start=True, stop=True)])
                    p.op("scalar", ["ps_b"], [f"nr{sl//2}"],
                         lambda a, sl=sl: a.activation(
                             yac[2+sl//2][0:1, 512*(sl % 2):512*(sl % 2)+512],
                             ps_b[0:1, 0:512], AF.Sqrt))
                p.op("vector", ["nr0"], ["col5"],
                     lambda v: v.reduce_max(col[5][0:1, :],
                                            yac[2][0:1, :], AX.X,
                                            negate=True))
                p.op("vector", ["nr1"], ["col6"],
                     lambda v: v.reduce_max(col[6][0:1, :],
                                            yac[3][0:1, :], AX.X,
                                            negate=True))
                p.op("vector", ["col5", "col6"], ["col5"],
                     lambda v: v.tensor_tensor(col[5][0:1, :],
                                               col[5][0:1, :],
                                               col[6][0:1, :], ALU.min))
                p.op("scalar", ["nr0", "col5"], ["rows_b", "col6"],
                     lambda a: a.activation(
                         rows_b[0:1, 0:1024], yac[2][0:1, 0:1024], AF.Exp,
                         bias=col[5][0:1, 0:1], accum_out=col[6][0:1, :]))
                p.op("scalar", ["nr1", "col5"], ["rows_b", "col7"],
                     lambda a: a.activation(
                         rows_b[0:1, 1024:2048], yac[3][0:1, 0:1024],
                         AF.Exp, bias=col[5][0:1, 0:1],
                         accum_out=col[7][0:1, :]))
                p.op("vector", ["rows_b"], ["nr0"],
                     lambda v: v.reduce_sum(
                         yac[2][0:1, 0:256],
                         rows_b[0:1, 0:1024].rearrange(
                             "o (l r) -> o l r", r=4), AX.X))
                p.op("vector", ["rows_b"], ["nr0"],
                     lambda v: v.reduce_sum(
                         yac[2][0:1, 256:512],
                         rows_b[0:1, 1024:2048].rearrange(
                             "o (l r) -> o l r", r=4), AX.X))
                p.op("vector", ["nr0"], ["nr0"],
                     lambda v: v.tensor_single_scalar(
                         yac[2][0:1, 0:512], yac[2][0:1, 0:512], 1e-8,
                         ALU.add))
                p.op("vector", ["nr0"], ["nr1"],
                     lambda v: v.reciprocal(yac[3][0:1, 0:512],
                                            yac[2][0:1, 0:512]))
                for sl in range(4):
                    p.mm_group(["rows_b", "ones_rb"], ["ps_b"], [
                        lambda t, sl=sl: t.matmul(
                            ps_b[0:64, 0:512], ones_rb[0:1, 0:64],
                            rows_b[0:1, 512*sl:512*(sl+1)],
                            start=True, stop=True)])
                    p.op("vector", ["ps_b", "scr2"], ["scr4"],
                         lambda v, sl=sl, sl_ap=sl_ap:
                         v.tensor_tensor(
                             scr[4][0:64, 512*sl:512*(sl+1)],
                             sl_ap[:, 512*sl:512*(sl+1)],
                             ps_b[0:64, 0:512], ALU.mult))
                    p.op("vector", ["scr4"], ["ybc0"],
                         lambda v, sl=sl: v.reduce_sum(
                             yac[0][0:64, 128*sl:128*(sl+1)],
                             scr[4][0:64, 512*sl:512*(sl+1)].rearrange(
                                 "p (l r) -> p l r", r=4), AX.X))
                p.mm_group(["nr1", "ones_f"], ["ps_b"], [
                    lambda t: t.matmul(
                        ps_b[0:64, 0:512], ones_f[0:1, 0:64],
                        yac[3][0:1, 0:512], start=True, stop=True)])
                p.op("vector", ["ps_b", "ybc0"], [dst_n],
                     lambda v, dst_ap=dst_ap: v.tensor_tensor(
                         dst_ap, yac[0][0:64, 0:512],
                         ps_b[0:64, 0:512], ALU.mult))

            for h in range(NH):
                if h == 8:
                    vfm_stage(1)
                h2 = 64 * (h % 2)
                qf_t, qf_n = qfm[h // 2]
                vf_t = pool[16 + (h % 8) // 2]
                vf_n = f"pl{16 + (h % 8)//2}"
                p.mm_group(["scr3", "ident_b"], ["ps_t0"], [
                    lambda t, h=h: t.transpose(
                        ps_t0[0:64, 0:128], scr[3][:, 64*h:64*(h+1)],
                        ident_b[:, :])])
                p.op("vector", ["ps_t0"], ["scr0"],
                     lambda v: v.tensor_copy(
                         scr[0][0:64, 0:128], ps_t0[0:64, 0:128]))
                p.mm_group(["scr3", "ident_b"], ["ps_t0"], [
                    lambda t, h=h: t.transpose(
                        ps_t0[0:64, 128:204],
                        scr[3][0:76, 1024+64*h:1024+64*(h+1)],
                        ident_b[0:76, 0:76])])
                p.op("vector", ["ps_t0"], ["scr0"],
                     lambda v: v.tensor_copy(
                         scr[0][0:64, 128:204],
                         ps_t0[0:64, 128:204]))
                kf_t, kf_n = kfm[h // 2]
                compress(kf_t, kf_n, h, scr[0][0:64, 204:716], "scr0")
                compress(vf_t, vf_n, h,
                         scr[0][0:64, 716:1228], "scr0")
                for ct in range(4):
                    p.mm_group(["scr0", "ident_b"], ["ps_t1"], [
                        lambda t, ct=ct: t.transpose(
                            ps_t1[0:128, 64*ct:64*(ct+1)],
                            scr[0][0:64, 716+128*ct:716+128*(ct+1)],
                            ident_b[0:64, 0:64])])
                    p.op("vector", ["ps_t1"], ["scr2"],
                         lambda v, ct=ct: v.tensor_copy(
                             scr[2][:, 1536+64*ct:1536+64*(ct+1)],
                             ps_t1[0:128, 64*ct:64*(ct+1)]))
                if h % 2 == 1:
                    p.op("vector", ["scr0"], ["scr0"],
                         lambda v: v.tensor_copy(
                             scr[0][64:128, 0:716], scr[0][0:64, 0:716]))
                for tq in range(NTI):
                    g2 = tq % 2
                    s2v = ps_big[:, 1024*g2:1024*g2+716]
                    s2n = f"psBs{g2}"
                    if g2 == 0:
                        p2 = scr[4][:, 0:716]
                        pT0 = 1024
                        pTt = scr[4]
                        pbn = "scr4"
                    else:
                        p2 = scr[2][:, 0:716]
                        pT0 = 768
                        pTt = scr[2]
                        pbn = "scr2"
                    p.mm_group(["scr0", qf_n], [s2n], [
                        lambda t, tq=tq, h2=h2, qf_t=qf_t, s2v=s2v:
                        t.matmul(s2v[:, 0:512],
                                 qf_t[h2:h2+64, 128*tq:128*(tq+1)],
                                 scr[0][h2:h2+64, 0:512],
                                 start=True, stop=True),
                        lambda t, tq=tq, h2=h2, qf_t=qf_t, s2v=s2v:
                        t.matmul(s2v[:, 512:716],
                                 qf_t[h2:h2+64, 128*tq:128*(tq+1)],
                                 scr[0][h2:h2+64, 512:716],
                                 start=True, stop=True)])
                    p.op("vector", [s2n], ["col0"],
                         lambda v, s2v=s2v: v.reduce_max(
                             col[0][:, :], s2v[:, :], AX.X, negate=True))
                    p.op("scalar", [s2n, "col0"], [pbn, "col1"],
                         lambda a, s2v=s2v, p2=p2: a.activation(
                             p2, s2v[:, :], AF.Exp,
                             bias=col[0][:, 0:1], accum_out=col[1][:, :]))
                    p.op("vector", ["col1"], ["col2"],
                         lambda v: v.reciprocal(col[2][:, :],
                                                col[1][:, :]))
                    p.op("vector", ["col2"], [pbn],
                         lambda v, p2=p2: v.tensor_scalar_mul(
                             p2, p2, col[2][:, 0:1]))
                    widths = [(0, 128), (128, 76), (204, 128), (332, 128),
                              (460, 128), (588, 128)]
                    for grp in range(2):
                        pst = ps_t0 if grp == 0 else ps_t1
                        pstn = f"ps_t{grp}"
                        fs = []
                        for cc in range(3):
                            ci = grp * 3 + cc
                            c0, w = widths[ci]
                            fs.append(lambda t, c0=c0, w=w, cc=cc, pst=pst,
                                      p2=p2: t.transpose(
                                          pst[0:w, 128*cc:128*(cc+1)],
                                          p2[:, c0:c0+w],
                                          ident_b[:, :]))
                        p.mm_group([pbn, "ident_b"], [pstn], fs)
                        p.op("vector", [pstn], [pbn],
                             lambda v, grp=grp, pst=pst, pTt=pTt, pT0=pT0:
                             v.tensor_copy(
                                 pTt[:, pT0+384*grp:pT0+384*(grp+1)],
                                 pst[:, 0:384]))
                    pon = "ps_a" if tq % 2 == 0 else "ps_b"
                    pov = PSV[pon]
                    vchunks = [
                        (scr[1][:, 64*h:64*(h+1)], "scr1", 128),
                        (scr[1][0:76, 1024+64*h:1024+64*(h+1)], "scr1", 76),
                        (scr[2][0:128, 1536:1600], "scr2", 128),
                        (scr[2][0:128, 1600:1664], "scr2", 128),
                        (scr[2][0:128, 1664:1728], "scr2", 128),
                        (scr[2][0:128, 1728:1792], "scr2", 128),
                    ]
                    fs = []
                    for ci, (vcap, vn, w) in enumerate(vchunks):
                        fs.append(lambda t, ci=ci, vcap=vcap, w=w, pov=pov,
                                  pTt=pTt, pT0=pT0: t.matmul(
                                      pov[0:64, 0:128], vcap[0:w, :],
                                      pTt[0:w, pT0+128*ci:
                                          pT0+128*(ci+1)],
                                      start=(ci == 0), stop=(ci == 5)))
                    p.mm_group([pbn, "scr1", "scr2"], [pon], fs)
                    p.op("vector", [pon], [qf_n],
                         lambda v, tq=tq, h2=h2, pov=pov, qf_t=qf_t:
                         v.tensor_copy(
                             qf_t[h2:h2+64, 128*tq:128*(tq+1)],
                             pov[0:64, 0:128]))
            p.fence()
            st = None
            for tc in range(NTI):
                j = tc % 2
                p.op("sync", [], [f"xio{j}"], lambda s, tc=tc, j=j:
                     s.dma_start(xio[j][:, :], src[128*tc:128*(tc+1), :]),
                     sem=f"dsx2{blk}", inc=16)
                for half in range(2):
                    psn = "ps_a" if half == 0 else "ps_b"
                    psv = PSV[psn]
                    fs = []
                    for ic in range(8):
                        fs.append(lambda t, ic=ic, tc=tc, half=half, psv=psv:
                                  t.matmul(
                                      psv[:, :],
                                      pool[ic][:, 128*tc:128*(tc+1)],
                                      wb[8+ic][:, 512*half:512*(half+1)],
                                      start=(ic == 0), stop=(ic == 7)))
                    p.mm_group([f"pl{i}" for i in range(8)] +
                               [f"wb{8+i}" for i in range(8)], [psn], fs)
                    p.op("vector", [psn, f"xio{j}"], ["xio2"],
                         lambda v, half=half, psv=psv, j=j: v.tensor_tensor(
                             xio[2][:, 512*half:512*(half+1)], psv[:, :],
                             xio[j][:, 512*half:512*(half+1)], ALU.add))
                st = p.op("sync", ["xio2"], [f"xout{blk}"],
                          lambda s, tc=tc: s.dma_start(
                              dst[128*tc:128*(tc+1), :], xio[2][:, :]),
                          sem=f"dso{blk}", inc=16)
            return st

        def emit_l1_ffn(src, dst, blk, src_tok=None):
            emit_ln(src, blk, src_tok)
            p.op("sync", [], ["scr3"], lambda s: s.dma_start(
                scr[3][0:8, 0:1024], gW1_b[:, :]), sem=f"dsw{blk}", inc=16,
                extra=[cast_tok["gW1"]])
            fs = []
            for c in range(8):
                fs.append(lambda t, c=c: t.transpose(
                    ps_t0[:, 8*c:8*(c+1)], scr[3][0:8, 128*c:128*(c+1)],
                    ident_b[0:8, 0:8]))
            p.mm_group(["scr3", "ident_b"], ["ps_t0"], fs)
            p.op("vector", ["ps_t0"], ["gwt"], lambda v: v.tensor_copy(
                gwt[:, :], ps_t0[:, 0:64]))
            for tc in range(NTI):
                fs = []
                for ic in range(8):
                    fs.append(lambda t, ic=ic, tc=tc: t.matmul(
                        ps_a[:, 0:8], xn[ic][:, 128*tc:128*(tc+1)],
                        gwt[:, 8*ic:8*(ic+1)],
                        start=(ic == 0), stop=(ic == 7)))
                p.mm_group(xn_names() + ["gwt"], ["ps_a"], fs)
                p.op("vector", ["ps_a"], ["col0"],
                     lambda v: v.reduce_max(col[0][:, :], ps_a[:, 0:8],
                                            AX.X, negate=True))
                p.op("scalar", ["ps_a", "col0"], [f"lg{tc}", "col1"],
                     lambda a, tc=tc: a.activation(
                         logit[tc][:, :], ps_a[:, 0:8], AF.Exp,
                         bias=col[0][:, 0:1], accum_out=col[1][:, :]))
                p.op("vector", ["col1"], ["col2"],
                     lambda v: v.reciprocal(col[2][:, :], col[1][:, :]))
                p.op("vector", ["col2", f"lg{tc}"], [f"lg{tc}"],
                     lambda v, tc=tc: v.tensor_scalar_mul(
                         logit[tc][:, :], logit[tc][:, :], col[2][:, 0:1]))
                for e in range(8):
                    p.op("vector", [f"lg{tc}"], ["cmp8x"],
                         lambda v, tc=tc, e=e: v.tensor_scalar(
                             cmp8[:, 8*e:8*(e+1)], logit[tc][:, :],
                             logit[tc][:, e:e+1], None, ALU.is_gt))
                    p.op("vector", [f"lg{tc}"], ["cmp8y"],
                         lambda v, tc=tc, e=e: v.tensor_scalar(
                             cmp8b[:, 8*e:8*(e+1)], logit[tc][:, :],
                             logit[tc][:, e:e+1], None, ALU.is_equal))
                p.op("vector", ["tri8", "cmp8y"], ["cmp8y"],
                     lambda v: v.tensor_tensor(cmp8b, cmp8b,
                                               tri8[:, :], ALU.mult))
                p.op("vector", ["cmp8y", "cmp8x"], ["cmp8x"],
                     lambda v: v.tensor_tensor(cmp8, cmp8,
                                               cmp8b, ALU.add))
                p.op("vector", ["cmp8x"], [f"mk{tc}"],
                     lambda v, tc=tc: v.reduce_sum(
                         msk[tc][:, :],
                         cmp8.rearrange("p (e j) -> p e j", j=8),
                         AX.X))
                p.op("vector", [], [f"mk{tc}"],
                     lambda v, tc=tc: v.tensor_single_scalar(
                         msk[tc][:, :], msk[tc][:, :], 1.5, ALU.is_lt))
                p.op("vector", [f"lg{tc}"], [f"mk{tc}"],
                     lambda v, tc=tc: v.tensor_tensor(
                         msk[tc][:, :], msk[tc][:, :], logit[tc][:, :],
                         ALU.mult))
                p.op("vector", [], [f"mk{tc}"],
                     lambda v, tc=tc: v.tensor_scalar_mul(
                         msk[tc][:, :], msk[tc][:, :], 0.5))

            st = None
            for ts in range(4):
                for e in range(NE):
                    for fcb in range(4):
                        f0 = fcb * 1024
                        for ic in range(8):
                            p.op("sync", [], [f"wb{ic}"],
                                 lambda s, ic=ic, e=e, f0=f0: s.dma_start(
                                     wb[ic][:, :],
                                     w1e_b[e, 128*ic:128*(ic+1),
                                           f0:f0+1024]),
                                 sem=f"dsw{blk}", inc=16,
                                 extra=[cast_tok["w1e"]])
                        for ic in range(8):
                            p.op("scalar", [], [f"wb{8+ic}"],
                                 lambda s, ic=ic, e=e, f0=f0: s.dma_start(
                                     wb[8+ic][:, :],
                                     w3e_b[e, 128*ic:128*(ic+1),
                                           f0:f0+1024]),
                                 sem=f"daw{blk}", inc=16,
                                 extra=[cast_tok["w3e"]])
                        for fc in range(8):
                            p.op("sync", [], [f"pl{16+fc//2}"],
                                 lambda s, fc=fc, e=e, f0=f0: s.dma_start(
                                     pool[16+fc//2][:, 1024*(fc % 2):
                                                    1024*(fc % 2)+1024],
                                     w2e_b[e, f0+128*fc:f0+128*(fc+1), :]),
                                 sem=f"dsw{blk}", inc=16,
                                 extra=[cast_tok["w2e"]])
                        for fc in range(8):
                            gt_ = pool[fc // 4]
                            goff = 512 * (fc % 4)
                            psn = "ps_a" if fc % 2 == 0 else "ps_b"
                            psv = PSV[psn]
                            fs = []
                            for ic in range(8):
                                fs.append(lambda t, ic=ic, fc=fc, ts=ts,
                                          psv=psv: t.matmul(
                                              psv[:, :],
                                              wb[ic][:, 128*fc:128*(fc+1)],
                                              xn[ic][:, 512*ts:512*(ts+1)],
                                              start=(ic == 0),
                                              stop=(ic == 7)))
                            p.mm_group(xn_names() +
                                       [f"wb{i}" for i in range(8)],
                                       [psn], fs)
                            p.op("scalar", [psn], [f"pl{fc//4}"],
                                 lambda a, gt_=gt_, goff=goff, psv=psv:
                                 a.activation(gt_[:, goff:goff+512],
                                              psv[:, :], AF.Silu))
                        for fc in range(8):
                            ut_ = pool[2 + fc // 4]
                            uoff = 512 * (fc % 4)
                            psn = "ps_a" if fc % 2 == 0 else "ps_b"
                            psv = PSV[psn]
                            fs = []
                            for ic in range(8):
                                fs.append(lambda t, ic=ic, fc=fc, ts=ts,
                                          psv=psv: t.matmul(
                                              psv[:, :],
                                              wb[8+ic][:, 128*fc:
                                                       128*(fc+1)],
                                              xn[ic][:, 512*ts:512*(ts+1)],
                                              start=(ic == 0),
                                              stop=(ic == 7)))
                            p.mm_group(xn_names() +
                                       [f"wb{8+i}" for i in range(8)],
                                       [psn], fs)
                            p.op("vector", [psn], [f"pl{2+fc//4}"],
                                 lambda v, ut_=ut_, uoff=uoff, psv=psv:
                                 v.tensor_copy(ut_[:, uoff:uoff+512],
                                               psv[:, :]))
                        for half in range(2):
                            p.op("vector", [f"pl{2+half}"], [f"pl{half}"],
                                 lambda v, half=half: v.tensor_tensor(
                                     pool[half][:, :], pool[half][:, :],
                                     pool[2+half][:, :], ALU.mult))
                        for tc2 in range(4):
                            tcg = ts * 4 + tc2
                            for dh in range(2):
                                pv = 2 * (tc2 % 2) + dh
                                psn = f"psB{pv}"
                                fs = []
                                for fc in range(8):
                                    gt_ = pool[fc // 4]
                                    goff = 512 * (fc % 4) + 128 * tc2
                                    fs.append(lambda t, fc=fc, gt_=gt_,
                                              goff=goff, dh=dh, pv=pv:
                                              t.matmul(
                                                  psBv(pv),
                                                  gt_[:, goff:goff+128],
                                                  pool[16+fc//2][
                                                      :, 1024*(fc % 2) +
                                                      512*dh:
                                                      1024*(fc % 2) +
                                                      512*(dh+1)],
                                                  start=(fc == 0),
                                                  stop=(fc == 7)))
                                p.mm_group(["pl0", "pl1"] +
                                           [f"pl{16+c}" for c in range(4)],
                                           [psn], fs)
                                first = (e == 0 and fcb == 0)
                                if first:
                                    p.op("vector", [psn, f"mk{tcg}"],
                                         [f"yac{tc2}"],
                                         lambda v, tc2=tc2, dh=dh, pv=pv,
                                         tcg=tcg, e=e: v.tensor_scalar(
                                             yac[tc2][:, 512*dh:512*(dh+1)],
                                             psBv(pv),
                                             msk[tcg][:, e:e+1], None,
                                             ALU.mult))
                                else:
                                    p.op("vector", [psn, f"mk{tcg}",
                                                    f"yac{tc2}"],
                                         [f"yac{tc2}"],
                                         lambda v, tc2=tc2, dh=dh, pv=pv,
                                         tcg=tcg, e=e:
                                         v.scalar_tensor_tensor(
                                             yac[tc2][:, 512*dh:512*(dh+1)],
                                             psBv(pv), msk[tcg][:, e:e+1],
                                             yac[tc2][:, 512*dh:512*(dh+1)],
                                             ALU.mult, ALU.add))
                for n in range(NE):
                    for ic in range(8):
                        p.op("sync", [], [f"wb{ic}"],
                             lambda s, ic=ic, n=n: s.dma_start(
                                 wb[ic][:, 0:512],
                                 w1s_b[n, 128*ic:128*(ic+1), :]),
                             sem=f"dsw{blk}", inc=16,
                             extra=[cast_tok["w1s"]])
                    for ic in range(8):
                        p.op("scalar", [], [f"wb{8+ic}"],
                             lambda s, ic=ic, n=n: s.dma_start(
                                 wb[8+ic][:, 0:512],
                                 w3s_b[n, 128*ic:128*(ic+1), :]),
                             sem=f"daw{blk}", inc=16,
                             extra=[cast_tok["w3s"]])
                    for fc in range(4):
                        p.op("sync", [], [f"pl{16+fc//2}"],
                             lambda s, fc=fc, n=n: s.dma_start(
                                 pool[16+fc//2][:, 1024*(fc % 2):
                                                1024*(fc % 2)+1024],
                                 w2s_b[n, 128*fc:128*(fc+1), :]),
                             sem=f"dsw{blk}", inc=16,
                             extra=[cast_tok["w2s"]])
                    for fc in range(4):
                        psn = "ps_a" if fc % 2 == 0 else "ps_b"
                        psv = PSV[psn]
                        fs = []
                        for ic in range(8):
                            fs.append(lambda t, ic=ic, fc=fc, ts=ts,
                                      psv=psv: t.matmul(
                                          psv[:, :],
                                          wb[ic][:, 128*fc:128*(fc+1)],
                                          xn[ic][:, 512*ts:512*(ts+1)],
                                          start=(ic == 0), stop=(ic == 7)))
                        p.mm_group(xn_names() +
                                   [f"wb{i}" for i in range(8)],
                                   [psn], fs)
                        p.op("scalar", [psn], ["pl0"],
                             lambda a, fc=fc, psv=psv: a.activation(
                                 pool[0][:, 512*fc:512*(fc+1)],
                                 psv[:, :], AF.Silu))
                    for fc in range(4):
                        psn = "ps_a" if fc % 2 == 0 else "ps_b"
                        psv = PSV[psn]
                        fs = []
                        for ic in range(8):
                            fs.append(lambda t, ic=ic, fc=fc, ts=ts,
                                      psv=psv: t.matmul(
                                          psv[:, :],
                                          wb[8+ic][:, 128*fc:128*(fc+1)],
                                          xn[ic][:, 512*ts:512*(ts+1)],
                                          start=(ic == 0), stop=(ic == 7)))
                        p.mm_group(xn_names() +
                                   [f"wb{8+i}" for i in range(8)],
                                   [psn], fs)
                        p.op("vector", [psn], ["pl1"],
                             lambda v, fc=fc, psv=psv: v.tensor_copy(
                                 pool[1][:, 512*fc:512*(fc+1)], psv[:, :]))
                    p.op("vector", ["pl1"], ["pl0"],
                         lambda v: v.tensor_tensor(
                             pool[0][:, :], pool[0][:, :], pool[1][:, :],
                             ALU.mult))
                    for tc2 in range(4):
                        for dh in range(2):
                            pv = 2 * (tc2 % 2) + dh
                            psn = f"psB{pv}"
                            fs = []
                            for fc in range(4):
                                goff = 512 * fc + 128 * tc2
                                fs.append(lambda t, fc=fc, goff=goff,
                                          dh=dh, pv=pv: t.matmul(
                                              psBv(pv),
                                              pool[0][:, goff:goff+128],
                                              pool[16+fc//2][
                                                  :, 1024*(fc % 2) +
                                                  512*dh:
                                                  1024*(fc % 2) +
                                                  512*(dh+1)],
                                              start=(fc == 0),
                                              stop=(fc == 3)))
                            p.mm_group(["pl0"] +
                                       [f"pl{16+c}" for c in range(2)],
                                       [psn], fs)
                            p.op("vector", [psn, f"yac{tc2}"],
                                 [f"yac{tc2}"],
                                 lambda v, tc2=tc2, dh=dh, pv=pv:
                                 v.scalar_tensor_tensor(
                                     yac[tc2][:, 512*dh:512*(dh+1)],
                                     psBv(pv), 1.0 / 16.0,
                                     yac[tc2][:, 512*dh:512*(dh+1)],
                                     ALU.mult, ALU.add))
                for tc2 in range(4):
                    tcg = ts * 4 + tc2
                    st = residual_store(src, dst, blk, tcg, f"yac{tc2}",
                                        yac[tc2][:, :])
            return st

        srcT = X
        src_tok = None
        emitters = {0: emit_l0_attn, 1: emit_l0_ffn,
                    2: emit_l1_attn, 3: emit_l1_ffn}
        for bi, b in enumerate(blocks):
            dstT = Y if bi == len(blocks) - 1 else xres[bi % 2]
            st = emitters[b](srcT, dstT, blk=b, src_tok=src_tok)
            p.fence()
            srcT = dstT
            src_tok = st

        p.barrier_wait("gpsimd", [src_tok])

        for name in list(p.cnt.keys()):
            p.sems[name] = E(nc.semaphore(name))
        blk_ctx = E(nc.Block())
        for eng in ENGINES:
            ops = p.q[eng]
            if not ops:
                continue
            def runner(e, ops=ops):
                for f in ops:
                    f(e)
            getattr(blk_ctx, eng)(runner)

    return nc


_NC_CACHE = {}

def _get_nc(blocks):
    key = tuple(blocks)
    if key not in _NC_CACHE:
        _NC_CACHE[key] = build_program(blocks)
    return _NC_CACHE[key]


def kernel(**inputs):
    x = np.ascontiguousarray(np.asarray(inputs["L0_attn_x"], dtype=np.float32))
    weights = {k: np.ascontiguousarray(np.asarray(v, dtype=np.float32))
               for k, v in inputs.items() if k != "L0_attn_x"}
    nc = _get_nc((0, 1, 2, 3))
    in_maps = []
    for c in range(8):
        m = {"x": x[c]}
        m.update(weights)
        in_maps.append(m)
    res = run_bass_kernel_spmd(nc, in_maps, core_ids=list(range(8)))
    return np.stack([np.asarray(r["y"]) for r in res.results], axis=0)
